# version 04
# baseline (speedup 1.0000x reference)
import numpy as np
import ml_dtypes

import concourse.bass as bass
import concourse.mybir as mybir
from concourse.bass_utils import run_bass_kernel_spmd

F32 = mybir.dt.float32
BF16 = mybir.dt.bfloat16
AF = mybir.ActivationFunctionType
ALU = mybir.AluOpType
AX = mybir.AxisListType
NPBF16 = ml_dtypes.bfloat16

PAIRS = [[0, 1], [2, 3], [4, 5], [6, 7]]
B, S, D = 4, 4096, 1024
NH, HD = 16, 64
T = 2048
NC = 8
KH = 33
GELU_A = 0.7978845608028654
GELU_B = 0.044715 * GELU_A

ENGINES = ("sync", "gpsimd", "tensor", "vector", "scalar")

_DEBUG_STOP = None
_DUMP_OFF = 0
_HL_TAP = None


def _conv_constants():
    w = np.arange(32)
    u = np.arange(128)
    lam = np.arange(128)
    c = {}
    ang1 = 2 * np.pi * np.outer(w, np.arange(KH)) / 64.0
    F1R, F1I = np.cos(ang1), -np.sin(ang1)
    dw = np.exp(-1.28 * w)[:, None]
    c["F1R"] = np.tile(F1R, (4, 1))
    c["F1I"] = np.tile(F1I, (4, 1))
    c["F1Rh"] = np.tile(F1R * dw, (4, 1))
    c["F1Ih"] = np.tile(F1I * dw, (4, 1))
    GR = np.zeros((KH, 128, 128))
    GI = np.zeros((KH, 128, 128))
    for k in range(KH):
        ang = 2 * np.pi * (np.outer(u, lam) / 128.0 + u[:, None] * k / 8192.0)
        GR[k], GI[k] = np.cos(ang), -np.sin(ang)
    du = np.exp(-0.01 * u)[None, :, None]
    c["GR"], c["GIneg"], c["GI"] = GR, -GI, GI
    c["GRh"], c["GInegh"], c["GIh"] = GR * du, -GI * du, GI * du
    G2R = np.zeros((KH, 128, 128))
    G2I = np.zeros((KH, 128, 128))
    for k in range(KH):
        ang = 2 * np.pi * (np.outer(lam, u) / 128.0 + u[None, :] * k / 8192.0)
        G2R[k], G2I[k] = np.cos(ang) / 8192.0, np.sin(ang) / 8192.0
    c["G2R"], c["G2I"], c["G2Ineg"] = G2R, G2I, -G2I
    wt = np.where((np.arange(KH) == 0) | (np.arange(KH) == 32), 1.0, 2.0)
    angw = 2 * np.pi * np.outer(np.arange(KH), w) / 64.0
    c["CwR"] = wt[:, None] * np.cos(angw)
    c["CwI"] = -wt[:, None] * np.sin(angw)
    return {k: np.ascontiguousarray(v.astype(NPBF16)) for k, v in c.items()}


def _pos_embT():
    positions = np.arange(S, dtype=np.float32)[:, None]
    div = np.exp(np.arange(0, HD, 2, dtype=np.float32) * -(np.log(10000.0) / HD))
    pe = np.zeros((S, HD), np.float32)
    pe[:, 0::2] = np.sin(positions * div)
    pe[:, 1::2] = np.cos(positions * div)
    return np.ascontiguousarray(pe.T.astype(NPBF16))


class Sem:
    __slots__ = ("h", "n")

    def __init__(self, nc, name):
        self.h = nc.alloc_semaphore(name)
        self.n = 0


class Prog:
    def __init__(self, nc):
        self.nc = nc
        self.ops = {e: [] for e in ENGINES}

    def sem(self, name):
        return Sem(self.nc, name)

    def op(self, eng, fn, inc=None, inc_n=1):
        if inc is not None:
            inc.n += inc_n
            self.ops[eng].append(
                lambda e, fn=fn, s=inc.h, n=inc_n: fn(e).then_inc(s, n))
        else:
            self.ops[eng].append(lambda e, fn=fn: fn(e))

    def wait(self, eng, sem, val=None):
        v = sem.n if val is None else val
        if v > 0:
            self.ops[eng].append(lambda e, s=sem.h, v=v: e.wait_ge(s, v))

    def dma(self, eng, out, in_, sem, slow_ok=False):
        if slow_ok:
            self.op(eng, lambda e, o=out, i=in_: e.dma_start(
                out=o, in_=i, allow_slow_non_contiguous=True),
                inc=sem, inc_n=16)
        else:
            self.op(eng, lambda e, o=out, i=in_: e.dma_start(out=o, in_=i),
                    inc=sem, inc_n=16)


def _bias_col_ap(t):
    return t.ap().rearrange("(c p) -> p c", p=128)


def _dup_col_ap(t):
    return t.ap().unsqueeze(0).broadcast_to([2, HD])


def build_program(debug_stop=None):
    global _HL_TAP
    nc = bass.Bass("TRN2", target_bir_lowering=False)
    CONST = _conv_constants()

    def ein(name, shape, dt=F32):
        return nc.dram_tensor(name, shape, dt, kind="ExternalInput")

    x_in = ein("x_slab", [T, D])
    Wq, Wk, Wv, Wo = (ein(n, [D, D]) for n in ("Wq", "Wk", "Wv", "Wo"))
    bq, bk, bv, bo = (ein(n, [D]) for n in ("bq", "bk", "bv", "bo"))
    qs_i, qb_i, ks_i, kb_i, vs_i, vb_i = (
        ein(n, [HD]) for n in ("q_scale", "q_bias", "k_scale", "k_bias",
                               "v_scale", "v_bias"))
    Wu, Wvb, Wob = (ein(n, [D, D]) for n in ("Wu", "Wvb", "Wob"))
    bu, bvb, bob = (ein(n, [D]) for n in ("bu", "bvb", "bob"))
    W1, W2 = ein("W1", [HD, HD]), ein("W2", [HD, HD])
    b1, b2 = ein("b1", [HD]), ein("b2", [HD])
    W3my, b3my = ein("W3my", [HD, 512]), ein("b3my", [512])
    Wf, f_alpha, f_bias = ein("Wf", [D, D]), ein("f_alpha", [1]), ein("f_bias", [D])
    sel = ein("sel", [2])

    out_t = nc.dram_tensor("out", [T, D], F32, kind="ExternalOutput")

    ic = {k: nc.inline_tensor(v, name=f"c_{k}") for k, v in CONST.items()}
    pe_c = nc.inline_tensor(_pos_embT(), name="c_posT")
    id_b_c = nc.inline_tensor(np.eye(128, dtype=NPBF16), name="c_idb")
    id_f_c = nc.inline_tensor(np.eye(128, dtype=np.float32), name="c_idf")
    id64x2_c = nc.inline_tensor(
        np.vstack([np.eye(64), np.eye(64)]).astype(NPBF16), name="c_id64x2")
    onesrow_f_c = nc.inline_tensor(np.ones((1, 128), np.float32), name="c_onesrow")
    ones2 = np.zeros((128, 2), np.float32)
    ones2[:64, 0] = 1.0
    ones2[64:, 1] = 1.0
    ones2_c = nc.inline_tensor(ones2.astype(NPBF16), name="c_ones2")
    sel128 = np.zeros((2, 128), np.float32)
    sel128[0, :64] = 1.0
    sel128[1, 64:] = 1.0
    sel128_c = nc.inline_tensor(sel128, name="c_sel128")
    onescol_b_c = nc.inline_tensor(np.ones((128, 1), NPBF16), name="c_onescol")
    onescol_f_c = nc.inline_tensor(np.ones((128, 1), np.float32), name="c_onescolf")
    drall_c = nc.inline_tensor(
        np.full((128, 128), 0.7 / float(D * D), np.float32), name="c_drall")

    def idram(name, shape, dt=BF16):
        return nc.dram_tensor(name, shape, dt, kind="Internal")

    d_K_in = idram("d_K_in", [D, T])
    d_K_out = idram("d_K_out", [2 * D, T])
    d_V_in = idram("d_V_in", [D, T])
    d_V_out = idram("d_V_out", [2 * D, T])
    d_vhin = idram("d_vhin", [T, D])
    d_vhout = idram("d_vhout", [S, D])
    d_y_in = idram("d_y_in", [S, 512])
    d_y_out = idram("d_y_out", [2 * S, 512])
    d_Yv = idram("d_Yv", [2, KH, 128, 512])
    d_Yh = idram("d_Yh", [2, KH, 128, 512])
    d_H = idram("d_H", [2, KH, 128, 512])
    d_B = idram("d_B", [2, KH, 128, 512])
    d_hT = idram("d_hT", [S, 512])
    d_rt = idram("d_rt", [128], F32)

    p = Prog(nc)
    sb = nc.alloc_sbuf_tensor

    xf = [sb(f"x{i}", [128, T], F32) for i in range(NC)]
    AR1 = sb("AR1", [128, 16384], BF16)
    AR2 = sb("AR2", [128, 16384], BF16)
    wA = sb("wA", [128, 8192], BF16)
    wT = sb("wT", [128, 8192], BF16)
    ATT = sb("ATT", [128, 16448], BF16)
    wstage = sb("wst", [128, D], F32)
    stat = sb("stat", [128, T], F32)
    ident_b = sb("ident_b", [128, 128], BF16)
    ident_f = sb("ident_f", [128, 128], F32)
    id64x2 = sb("id64x2", [128, 64], BF16)
    cols = sb("cols", [128, 64], F32)
    small = sb("small", [128, 32], F32)
    smallb = None
    ones2_s = sb("ones2_s", [128, 2], BF16)
    sel128_s = sb("sel128_s", [2, 128], F32)
    onesrow_s = sb("onesrow_s", [1, 128], F32)
    onescol_s = sb("onescol_s", [128, 1], BF16)
    onescol_f = sb("onescol_f", [128, 1], F32)
    drall_s = sb("drall_s", [128, 128], F32)

    def chunks(t, n, width):
        return [t[:, width * i: width * i + width] for i in range(n)]

    xh = chunks(AR1, NC, T)
    qh = chunks(AR2, NC, T)
    smallb = small.ap().bitcast(BF16)
    stg = AR2.bitcast(F32)[:, 0:2048]
    ATTf = ATT.bitcast(F32)
    st3 = ATTf[:, 6144:8192]
    statb = stat.ap().bitcast(BF16)
    wAc = chunks(wA, NC, D)
    wTc = chunks(wT, NC, D)
    vpair = ATT[:, 0:4096]
    kbufH = [ATT[:, 4096:6144], ATT[:, 6144:8192]]
    pbuf = [ATT[:, 8192:10240], ATT[:, 10240:12288]]
    vts = ATT[:, 12288:16448]
    kvst = [ATT[:, 0:2048], ATT[:, 2048:4096]]
    pe_s = ATT[:, 0:4096]
    hg1 = ATT[:, 4096:8192]
    hg2 = ATT[:, 8192:12288]
    w64 = ATT[:, 12288:12928]
    hstage = ATT[:, 0:512]
    vstage = ATT[:, 1024:2048]
    bvb_row = ATT[:, 0:1024]
    b3_row = ATT[:, 15872:16384]
    spec = ATT[:, 8192:10240]
    gmat = [ATT[:, 14336:15104], ATT[:, 15104:15872]]
    cstA = ATT[:, 0:4096]
    cstB = ATT[:, 4096:8192]
    cw1 = ATT[:, 10240:10752]
    cw2 = ATT[:, 10752:11264]
    cw3 = ATT[:, 11264:11776]
    cw4 = ATT[:, 11776:12288]
    bsp = [ATT[:, 12288:13312], ATT[:, 13312:14336]]
    cube = AR1

    PS = nc.alloc_psum_tensor("PS", [128, 4096], F32)
    PSb = PS.ap().bitcast(BF16)

    sems = {e: p.sem(f"pe_{e}") for e in ENGINES}
    s_ld = p.sem("ld")
    s_go = p.sem("go")
    s_cc = p.sem("cc")
    s_mm = p.sem("mm")
    s_ve = p.sem("ve")
    s_se = p.sem("se")
    s_qk = p.sem("qk")
    s_exp = p.sem("exp")
    s_pv = p.sem("pv")
    s_vt = p.sem("vt")

    def bar(producers=ENGINES, consumers=ENGINES):
        vals = {}
        for pr in producers:
            sm = sems[pr]
            sm.n += 1
            p.ops[pr].append(lambda e, h=sm.h: e.sem_inc(h, 1))
            vals[pr] = sm.n
        for cn in consumers:
            for pr in producers:
                if pr != cn:
                    p.wait(cn, sems[pr], vals[pr])

    def finish_output_from_xf():
        bar()
        for tt in range(16):
            p.wait("tensor", s_ve, s_ve.n)
            for c in range(NC):
                p.op("tensor", lambda e, c=c, tt=tt: e.transpose(
                    PS[:, 128 * c:128 * c + 128],
                    xf[c][:, 128 * tt:128 * tt + 128], ident_f[:, :]),
                    inc=(s_mm if c == NC - 1 else None))
            p.wait("vector", s_mm, s_mm.n)
            half = tt % 2
            p.op("vector", lambda e, half=half: e.tensor_copy(
                out=stg[:, 1024 * half:1024 * half + 1024], in_=PS[:, 0:1024]),
                inc=s_ve)
            p.wait("gpsimd", s_ve, s_ve.n)
            if tt >= 2:
                pass
            p.dma("gpsimd", out_t.ap()[128 * tt:128 * tt + 128, :],
                  stg[:, 1024 * half:1024 * half + 1024], s_go)
            p.wait("vector", s_go, s_go.n)
        p.wait("gpsimd", s_go, s_go.n)

    p.dma("sync", ident_b[:, :], id_b_c.ap()[:, :], s_ld)
    p.dma("sync", ident_f[:, :], id_f_c.ap()[:, :], s_ld)
    p.dma("sync", id64x2[:, :], id64x2_c.ap()[:, :], s_ld)
    p.dma("sync", ones2_s[:, :], ones2_c.ap()[:, :], s_ld)
    p.dma("sync", sel128_s[:, :], sel128_c.ap()[:, :], s_ld)
    p.dma("sync", onesrow_s[:, :], onesrow_f_c.ap()[:, :], s_ld)
    p.dma("sync", onescol_s[:, :], onescol_b_c.ap()[:, :], s_ld)
    p.dma("sync", onescol_f[:, :], onescol_f_c.ap()[:, :], s_ld)
    p.dma("sync", drall_s[:, :], drall_c.ap()[:, :], s_ld)
    for i, t in enumerate((bq, bk, bv, bo, bu, bvb, bob, f_bias)):
        p.dma("sync", cols[:, 8 * i:8 * i + 8], _bias_col_ap(t), s_ld,
              slow_ok=True)
    for i, t in enumerate((qs_i, qb_i, ks_i, kb_i, vs_i, vb_i)):
        p.dma("sync", small[:, i:i + 1], _dup_col_ap(t), s_ld, slow_ok=True)
    p.dma("sync", small[:, 6:7],
          f_alpha.ap().unsqueeze(0).broadcast_to([128, 1]), s_ld, slow_ok=True)
    p.dma("sync", small[:, 7:8],
          sel.ap()[0:1].unsqueeze(0).broadcast_to([128, 1]), s_ld, slow_ok=True)
    p.dma("sync", small[:, 8:9],
          sel.ap()[1:2].unsqueeze(0).broadcast_to([128, 1]), s_ld, slow_ok=True)
    p.dma("sync", small[:, 10:11], _dup_col_ap(b1), s_ld, slow_ok=True)
    p.dma("sync", small[:, 11:12], _dup_col_ap(b2), s_ld, slow_ok=True)
    p.dma("gpsimd", b3_row[:, :],
          b3my.ap().unsqueeze(0).broadcast_to([128, 512]), s_go)
    p.dma("sync", wstage[0:64, 0:64], W1.ap()[:, :], s_ld)
    p.dma("sync", wstage[0:64, 64:128], W2.ap()[:, :], s_ld)
    p.dma("sync", wstage[0:64, 128:640], W3my.ap()[:, :], s_ld)
    p.wait("vector", s_ld, s_ld.n)
    p.op("vector", lambda e: e.tensor_copy(out=w64[0:64, :],
                                           in_=wstage[0:64, 0:640]),
         inc=s_ve)

    x_tiles = x_in.ap().rearrange("(n q) d -> n q d", q=128)
    trans_done = []
    evac_done = []
    p.wait("sync", s_ve, s_ve.n)
    for tt in range(16):
        dst = wstage
        if tt >= 1:
            p.wait("sync", s_mm, trans_done[tt - 1])
        p.dma("sync", dst[:, :], x_tiles[tt], s_ld)
        p.wait("tensor", s_ld, s_ld.n)
        if tt >= 1:
            p.wait("tensor", s_ve, evac_done[tt - 1])
        for c in range(NC):
            p.op("tensor", lambda e, c=c, dst=dst: e.transpose(
                PS[:, 128 * c:128 * c + 128], dst[:, 128 * c:128 * c + 128],
                ident_f[:, :]), inc=(s_mm if c == NC - 1 else None))
        trans_done.append(s_mm.n)
        p.wait("vector", s_mm, s_mm.n)
        for c in range(NC):
            p.op("vector", lambda e, c=c, tt=tt: e.tensor_copy(
                out=xf[c][:, 128 * tt:128 * tt + 128],
                in_=PS[:, 128 * c:128 * c + 128]),
                inc=(s_ve if c == NC - 1 else None))
        evac_done.append(s_ve.n)

    ar1f = AR1.bitcast(F32)
    ar2f = AR2.bitcast(F32)

    def wf_rows(c):
        a = ar1f if c < 4 else ar2f
        return a[:, 1024 * (c % 4):1024 * (c % 4) + 1024]

    for c in range(NC):
        p.dma("sync", wf_rows(c), Wf.ap()[128 * c:128 * c + 128, :], s_ld)
    accA = stat[:, 0:1024]
    accB = stat[:, 1024:2048]
    scrA = AR1.bitcast(F32)[:, 4096:5120]
    p.op("gpsimd", lambda e: e.memset(stat[:, 128:256], 0.0),
         inc=sems["gpsimd"])
    p.wait("scalar", s_ld, s_ld.n)
    p.op("scalar", lambda e: e.activation(
        out=accA, in_=wf_rows(0), func=AF.Abs), inc=s_se)
    for c in range(1, NC):
        src_acc, dst_acc = (accA, accB) if c % 2 == 1 else (accB, accA)
        p.wait("scalar", s_se, s_se.n)
        p.op("scalar", lambda e, c=c: e.activation(
            out=scrA, in_=wf_rows(c), func=AF.Abs), inc=s_se)
        p.wait("vector", s_se, s_se.n)
        p.op("vector", lambda e, s=src_acc, d=dst_acc: e.tensor_tensor(
            out=d, in0=s, in1=scrA, op=ALU.add), inc=s_ve)
        p.wait("scalar", s_ve, s_ve.n)
    facc = accB if NC % 2 == 0 else accA
    p.wait("vector", sems["gpsimd"], sems["gpsimd"].n)
    p.op("vector", lambda e, a=facc: e.tensor_reduce(
        out=stat[:, 128:129], in_=a, axis=AX.X, op=ALU.add), inc=s_ve)
    p.wait("tensor", s_ve, s_ve.n)
    p.wait("tensor", s_ld, s_ld.n)
    p.op("tensor", lambda e: e.matmul(PS[:, 1024:1152], drall_s[:, :],
                                      stat[:, 128:256], start=True,
                                      stop=True), inc=None)
    p.op("tensor", lambda e: e.matmul(PS[0:128, 1152:1280], ident_b[:, :],
                                      ident_b[:, :], start=True,
                                      stop=True), inc=s_mm)
    p.wait("vector", s_mm, s_mm.n)
    p.op("vector", lambda e: e.tensor_copy(
        out=stat[:, 256:384], in_=PS[:, 1024:1152]), inc=s_ve)
    delta_col = stat[:, 256:257]
    if debug_stop == "tern":
        p.op("vector", lambda e: e.tensor_copy(
            out=xf[0][:, 0:32], in_=small[:, 0:32]), inc=None)
        p.op("vector", lambda e: e.tensor_copy(
            out=xf[1][:, 0:128], in_=stat[:, 256:384]), inc=None)
        p.op("vector", lambda e: e.tensor_copy(
            out=xf[3][:, 0:128], in_=PS[:, 1024:1152]), inc=None)
        p.op("vector", lambda e: e.tensor_copy(
            out=xf[4][:, 0:128], in_=drall_s[:, :]), inc=None)
        p.op("vector", lambda e: e.tensor_copy(
            out=xf[5][:, 0:128], in_=stat[:, 128:256]), inc=s_ve)
        finish_output_from_xf()
        return nc, p
    for c in range(NC):
        s1 = ATTf[:, 0:1024]
        s2 = ATTf[:, 1024:2048]
        p.wait("scalar", s_ve, s_ve.n)
        p.op("scalar", lambda e, c=c, o=s1: e.activation(
            out=o, in_=wf_rows(c), func=AF.Abs), inc=s_se)
        p.wait("vector", s_se, s_se.n)
        p.op("vector", lambda e, i=s1, o=s2: e.tensor_scalar(
            out=o, in0=i, scalar1=delta_col, scalar2=None,
            op0=ALU.is_gt), inc=s_ve)
        p.wait("scalar", s_ve, s_ve.n)
        p.op("scalar", lambda e, c=c, o=s1: e.sign(out=o, in_=wf_rows(c)),
             inc=s_se)
        p.wait("vector", s_se, s_se.n)
        p.op("vector", lambda e, c=c: e.tensor_tensor(
            out=wAc[c][:, :], in0=ATTf[:, 0:1024], in1=ATTf[:, 1024:2048],
            op=ALU.mult), inc=s_ve)
    p.wait("tensor", s_ve, s_ve.n)
    for c in range(NC):
        for r in range(NC):
            p.op("tensor", lambda e, c=c, r=r: e.transpose(
                PSb[:, 256 * r:256 * r + 128],
                wAc[r][:, 128 * c:128 * c + 128], ident_b[:, :]),
                inc=(s_mm if r == NC - 1 else None))
        p.wait("vector", s_mm, s_mm.n)
        for r in range(NC):
            p.op("vector", lambda e, c=c, r=r: e.tensor_copy(
                out=wTc[c][:, 128 * r:128 * r + 128],
                in_=PSb[:, 256 * r:256 * r + 128]),
                inc=(s_ve if r == NC - 1 else None))
        p.wait("tensor", s_ve, s_ve.n)
    bar()

    wload_done = []

    def load_weight(wsrc, dstc=None):
        dstc = dstc or wAc
        base = len(wload_done)
        for c in range(NC):
            st = wstage
            if base + c - 1 >= 0:
                p.wait("sync", s_ve, wload_done[base + c - 1])
            p.dma("sync", st[:, :], wsrc.ap()[128 * c:128 * c + 128, :], s_ld)
            p.wait("vector", s_ld, s_ld.n)
            p.op("vector", lambda e, c=c, st=st, d=dstc: e.tensor_copy(
                out=d[c][:, :], in_=st[:, :]), inc=s_ve)
            wload_done.append(s_ve.n)

    def layernorm_to(dst, eps=1e-5):
        for c in range(NC):
            p.op("vector", lambda e, c=c, d=dst: e.tensor_copy(
                out=d[c][:, :], in_=xf[c][:, :]), inc=None)
            p.op("vector", lambda e, c=c, d=dst: e.tensor_tensor(
                out=spec[:, :], in0=d[c][:, :], in1=d[c][:, :],
                op=ALU.mult), inc=s_ve)
            p.wait("tensor", s_ve, s_ve.n)
            for nb in range(4):
                sl = slice(512 * nb, 512 * nb + 512)
                p.op("tensor", lambda e, c=c, sl=sl, d=dst: e.matmul(
                    PS[0:1, 2048:4096][:, sl], onescol_s[:, :], d[c][:, sl],
                    start=(c == 0), stop=(c == NC - 1)), inc=None)
                p.op("tensor", lambda e, c=c, sl=sl: e.matmul(
                    PS[0:1, 0:2048][:, sl], onescol_s[:, :], spec[:, sl],
                    start=(c == 0), stop=(c == NC - 1)), inc=s_mm)
            p.wait("vector", s_mm, s_mm.n)
        p.op("vector", lambda e: e.tensor_scalar(
            out=stat[0:1, :], in0=PS[0:1, 2048:4096], scalar1=1.0 / D,
            scalar2=None, op0=ALU.mult), inc=None)
        p.op("vector", lambda e, eps=eps: e.tensor_scalar(
            out=st3[0:1, :], in0=PS[0:1, 0:2048], scalar1=1.0 / D,
            scalar2=float(eps), op0=ALU.mult, op1=ALU.add), inc=None)
        p.op("vector", lambda e: e.tensor_tensor(
            out=PS[0:1, 0:2048], in0=stat[0:1, :], in1=stat[0:1, :],
            op=ALU.mult), inc=None)
        p.op("vector", lambda e: e.scalar_tensor_tensor(
            out=PS[0:1, 2048:4096], in0=PS[0:1, 0:2048], scalar=-1.0,
            in1=st3[0:1, :], op0=ALU.mult, op1=ALU.add), inc=s_ve)
        p.wait("scalar", s_ve, s_ve.n)
        p.op("scalar", lambda e: e.activation(
            out=st3[0:1, :], in_=PS[0:1, 2048:4096], func=AF.Sqrt), inc=s_se)
        p.wait("vector", s_se, s_se.n)
        p.op("vector", lambda e: e.reciprocal(
            out=PS[0:1, 0:2048], in_=st3[0:1, :]), inc=None)
        p.op("vector", lambda e: e.tensor_copy(
            out=st3[0:1, :], in_=PS[0:1, 0:2048]), inc=s_ve)
        p.wait("tensor", s_ve, s_ve.n)
        for nb in range(4):
            sl = slice(512 * nb, 512 * nb + 512)
            p.op("tensor", lambda e, sl=sl: e.matmul(
                PS[:, 0:2048][:, sl], onesrow_s[0:1, :], stat[0:1, sl],
                start=True, stop=True), inc=None)
            p.op("tensor", lambda e, sl=sl: e.matmul(
                PS[:, 2048:4096][:, sl], onesrow_s[0:1, :], st3[0:1, sl],
                start=True, stop=True), inc=(s_mm if nb == 3 else None))
        p.wait("vector", s_mm, s_mm.n)
        for c in range(NC):
            p.op("vector", lambda e, c=c: e.tensor_tensor(
                out=stg[:, :], in0=xf[c][:, :], in1=PS[:, 0:2048],
                op=ALU.subtract), inc=None)
            p.op("vector", lambda e, c=c, d=dst: e.tensor_tensor(
                out=d[c][:, :], in0=stg[:, :], in1=PS[:, 2048:4096],
                op=ALU.mult), inc=s_ve)

    def residual_add(oc, bias_col, alpha_col=None):
        p.wait("vector", s_mm, s_mm.n)
        if alpha_col is not None:
            p.op("vector", lambda e, b=bias_col, a=alpha_col: e.tensor_scalar(
                out=stg[:, :], in0=PS[:, 0:2048], scalar1=a, scalar2=b,
                op0=ALU.mult, op1=ALU.add), inc=None)
        else:
            p.op("vector", lambda e, b=bias_col: e.tensor_scalar(
                out=stg[:, :], in0=PS[:, 0:2048], scalar1=b, scalar2=None,
                op0=ALU.add), inc=None)
        p.op("vector", lambda e, oc=oc: e.tensor_tensor(
            out=PS[:, 2048:4096], in0=stg[:, :], in1=xf[oc][:, :],
            op=ALU.add), inc=None)
        p.op("vector", lambda e, oc=oc: e.tensor_copy(
            out=xf[oc][:, :], in_=PS[:, 2048:4096]), inc=s_ve)

    def proj_chunks(xsrc, emit_out, wbuf=None):
        wbuf = wbuf if wbuf is not None else wAc
        for oc in range(NC):
            p.wait("tensor", s_ve, s_ve.n)
            for nb in range(4):
                sl = slice(512 * nb, 512 * nb + 512)
                for icc in range(NC):
                    p.op("tensor", lambda e, oc=oc, icc=icc, sl=sl, w=wbuf,
                         xs=xsrc: e.matmul(
                             PS[:, 0:2048][:, sl],
                             w[icc][:, 128 * oc:128 * oc + 128],
                             xs[icc][:, sl], start=(icc == 0),
                             stop=(icc == NC - 1)),
                         inc=(s_mm if (icc == NC - 1 and nb == 3) else None))
            emit_out(oc)

    def head_ln_out(oc, bias_off, sc, sbc, dst_tile, eps=1e-6):
        p.wait("vector", s_mm, s_mm.n)
        qb = pbuf[0]
        sqb = pbuf[1]
        p.op("vector", lambda e, oc=oc, off=bias_off: e.tensor_scalar(
            out=qb, in0=PS[:, 0:2048],
            scalar1=cols[:, off + oc:off + oc + 1], scalar2=None,
            op0=ALU.add), inc=None)
        p.op("vector", lambda e: e.tensor_tensor(
            out=sqb, in0=qb, in1=qb, op=ALU.mult), inc=s_ve)
        p.wait("tensor", s_ve, s_ve.n)
        for nb in range(4):
            sl = slice(512 * nb, 512 * nb + 512)
            p.op("tensor", lambda e, sl=sl: e.matmul(
                PS[0:2, 2048:4096][:, sl], ones2_s[:, :], qb[:, sl],
                start=True, stop=True), inc=None)
            p.op("tensor", lambda e, sl=sl: e.matmul(
                PS[0:2, 0:2048][:, sl], ones2_s[:, :], sqb[:, sl],
                start=True, stop=True), inc=(s_mm if nb == 3 else None))
        p.wait("vector", s_mm, s_mm.n)
        p.op("vector", lambda e: e.tensor_scalar(
            out=stat[0:2, :], in0=PS[0:2, 2048:4096], scalar1=1.0 / HD,
            scalar2=None, op0=ALU.mult), inc=None)
        p.op("vector", lambda e, eps=eps: e.tensor_scalar(
            out=st3[0:2, :], in0=PS[0:2, 0:2048], scalar1=1.0 / HD,
            scalar2=float(eps), op0=ALU.mult, op1=ALU.add), inc=None)
        p.op("vector", lambda e: e.tensor_tensor(
            out=PS[0:2, 0:2048], in0=stat[0:2, :], in1=stat[0:2, :],
            op=ALU.mult), inc=None)
        p.op("vector", lambda e: e.scalar_tensor_tensor(
            out=PS[0:2, 2048:4096], in0=PS[0:2, 0:2048], scalar=-1.0,
            in1=st3[0:2, :], op0=ALU.mult, op1=ALU.add), inc=s_ve)
        p.wait("scalar", s_ve, s_ve.n)
        p.op("scalar", lambda e: e.activation(
            out=st3[0:2, :], in_=PS[0:2, 2048:4096], func=AF.Sqrt), inc=s_se)
        p.wait("vector", s_se, s_se.n)
        p.op("vector", lambda e: e.reciprocal(
            out=PS[0:2, 0:2048], in_=st3[0:2, :]), inc=None)
        p.op("vector", lambda e: e.tensor_copy(
            out=st3[0:2, :], in_=PS[0:2, 0:2048]), inc=s_ve)
        p.wait("tensor", s_ve, s_ve.n)
        for nb in range(4):
            sl = slice(512 * nb, 512 * nb + 512)
            p.op("tensor", lambda e, sl=sl: e.matmul(
                PS[:, 0:2048][:, sl], sel128_s[:, :], stat[0:2, sl],
                start=True, stop=True), inc=None)
            p.op("tensor", lambda e, sl=sl: e.matmul(
                PS[:, 2048:4096][:, sl], sel128_s[:, :], st3[0:2, sl],
                start=True, stop=True), inc=(s_mm if nb == 3 else None))
        p.wait("vector", s_mm, s_mm.n)
        if _HL_TAP == "mu":
            p.op("vector", lambda e, d=dst_tile: e.tensor_copy(
                out=d, in_=PS[:, 0:2048]), inc=s_ve)
            return
        if _HL_TAP == "inv":
            p.op("vector", lambda e, d=dst_tile: e.tensor_copy(
                out=d, in_=PS[:, 2048:4096]), inc=s_ve)
            return
        p.op("vector", lambda e: e.tensor_tensor(
            out=sqb, in0=qb, in1=PS[:, 0:2048], op=ALU.subtract), inc=None)
        p.op("vector", lambda e: e.tensor_tensor(
            out=qb, in0=sqb, in1=PS[:, 2048:4096], op=ALU.mult),
            inc=None)
        if _HL_TAP == "norm":
            p.op("vector", lambda e, d=dst_tile: e.tensor_copy(
                out=d, in_=qb), inc=s_ve)
            return
        p.op("vector", lambda e, d=dst_tile, sc=sc, sbc=sbc: e.tensor_scalar(
            out=d, in0=qb, scalar1=small[:, sc:sc + 1],
            scalar2=small[:, sbc:sbc + 1], op0=ALU.mult, op1=ALU.add),
            inc=s_ve)

    def dump_dram(tile_srcs):
        bar()
        p.wait("sync", s_go, s_go.n)
        for tt in range(16):
            for src, (c0, c1) in tile_srcs(tt):
                p.dma("sync", spec[:, c0:c1], src, s_ld)
            p.wait("vector", s_ld, s_ld.n)
            p.op("vector", lambda e: e.tensor_copy(
                out=stg[:, 0:1024], in_=spec[:, 0:1024]), inc=s_ve)
            p.wait("gpsimd", s_ve, s_ve.n)
            p.dma("gpsimd", out_t.ap()[128 * tt:128 * tt + 128, :],
                  stg[:, 0:1024], s_go)
            p.wait("sync", s_go, s_go.n)
        p.wait("gpsimd", s_go, s_go.n)

    def dump_seq512(d):
        v2 = d.ap().rearrange("(a two) c -> a (two c)", two=2)
        dump_dram(lambda tt: [(v2[128 * tt:128 * tt + 128, :], (0, 1024))])

    def dump_kcube(d):
        dump_dram(lambda tt: [(d.ap()[0, _DUMP_OFF + tt], (0, 512)),
                              (d.ap()[1, _DUMP_OFF + tt], (512, 1024))])

    p.dma("sync", pe_s[0:64, :], pe_c.ap()[:, :], s_ld)
    p.dma("sync", wstage[0:64, 0:64], W1.ap()[:, :], s_ld)
    p.dma("sync", wstage[0:64, 64:128], W2.ap()[:, :], s_ld)
    p.dma("sync", wstage[0:64, 128:640], W3my.ap()[:, :], s_ld)
    p.wait("vector", s_ld, s_ld.n)
    p.op("vector", lambda e: e.tensor_copy(out=w64[0:64, :],
                                           in_=wstage[0:64, 0:640]),
         inc=s_ve)
    p.dma("gpsimd", b3_row[:, :],
          b3my.ap().unsqueeze(0).broadcast_to([128, 512]), s_go)
    p.wait("vector", s_go, s_go.n)
    p.wait("tensor", s_ld, s_ld.n)
    p.wait("tensor", s_ve, s_ve.n)

    def gelu_blocks(dst, bcol):
        for nb in range(8):
            sl = slice(512 * nb, 512 * nb + 512)
            g = stat[0:64, 0:512]
            s1 = stat[0:64, 512:1024]
            s2 = stat[0:64, 1024:1536]
            p.wait("vector", s_mm, s_mm.n)
            p.op("vector", lambda e, sl=sl, g=g, bcol=bcol: e.tensor_scalar(
                out=g, in0=PS[0:64, 0:4096][:, sl], scalar1=bcol,
                scalar2=None, op0=ALU.add), inc=None)
            p.op("vector", lambda e, g=g, s1=s1: e.tensor_tensor(
                out=s1, in0=g, in1=g, op=ALU.mult), inc=None)
            p.op("vector", lambda e, s1=s1, s2=s2: e.tensor_scalar(
                out=s2, in0=s1, scalar1=GELU_B, scalar2=GELU_A,
                op0=ALU.mult, op1=ALU.add), inc=None)
            p.op("vector", lambda e, g=g, s1=s1, s2=s2: e.tensor_tensor(
                out=s1, in0=s2, in1=g, op=ALU.mult), inc=s_ve)
            p.wait("scalar", s_ve, s_ve.n)
            p.op("scalar", lambda e, s1=s1, s2=s2: e.activation(
                out=s2, in_=s1, func=AF.Tanh), inc=s_se)
            p.wait("vector", s_se, s_se.n)
            p.op("vector", lambda e, s1=s1, s2=s2: e.tensor_scalar(
                out=s1, in0=s2, scalar1=0.5, scalar2=0.5,
                op0=ALU.mult, op1=ALU.add), inc=None)
            p.op("vector", lambda e, sl=sl, g=g, s1=s1, dst=dst:
                 e.tensor_tensor(out=dst[0:64, :][:, sl], in0=s1, in1=g,
                                 op=ALU.mult), inc=s_ve)

    for nb in range(8):
        sl = slice(512 * nb, 512 * nb + 512)
        p.op("tensor", lambda e, sl=sl: e.matmul(
            PS[0:64, 0:4096][:, sl], w64[0:64, 0:64], pe_s[0:64, :][:, sl],
            start=True, stop=True), inc=(s_mm if nb == 7 else None))
    gelu_blocks(hg1, small[0:64, 10:11])
    p.wait("tensor", s_ve, s_ve.n)
    for nb in range(8):
        sl = slice(512 * nb, 512 * nb + 512)
        p.op("tensor", lambda e, sl=sl: e.matmul(
            PS[0:64, 0:4096][:, sl], w64[0:64, 64:128], hg1[0:64, :][:, sl],
            start=True, stop=True), inc=(s_mm if nb == 7 else None))
    gelu_blocks(hg2, small[0:64, 11:12])
    if debug_stop == "fmlp":
        for c in range(NC):
            p.op("vector", lambda e, c=c: e.memset(xf[c][:, :], 0.0),
                 inc=None)
        p.op("vector", lambda e: e.tensor_copy(
            out=xf[0][:, :], in_=hg1[:, 0:2048]), inc=None)
        p.op("vector", lambda e: e.tensor_copy(
            out=xf[1][:, :], in_=hg1[:, 2048:4096]), inc=None)
        p.op("vector", lambda e: e.tensor_copy(
            out=xf[2][:, :], in_=hg2[:, 0:2048]), inc=None)
        p.op("vector", lambda e: e.tensor_copy(
            out=xf[3][:, :], in_=hg2[:, 2048:4096]), inc=None)
        p.op("vector", lambda e: e.tensor_copy(
            out=xf[4][0:64, 0:640], in_=w64[0:64, :]), inc=None)
        p.op("vector", lambda e: e.tensor_copy(
            out=xf[5][:, :], in_=pe_s[:, 0:2048]), inc=None)
        p.op("vector", lambda e: e.tensor_copy(
            out=xf[6][:, :], in_=pe_s[:, 2048:4096]), inc=None)
        p.op("vector", lambda e: e.tensor_copy(
            out=xf[7][:, 0:32], in_=small[:, 0:32]), inc=s_ve)
        finish_output_from_xf()
        return nc, p
    p.wait("tensor", s_ve, s_ve.n)
    for tc in range(32):
        p.op("tensor", lambda e, tc=tc: e.matmul(
            PS[:, 2048:2560], hg2[0:64, 128 * tc:128 * tc + 128],
            w64[0:64, 128:640], start=True, stop=True), inc=s_mm)
        p.wait("vector", s_mm, s_mm.n)
        p.op("vector", lambda e: e.scalar_tensor_tensor(
            out=hstage[:, :], in0=PS[:, 2048:2560], scalar=1.0,
            in1=b3_row[:, :], op0=ALU.mult, op1=ALU.add), inc=s_ve)
        p.wait("gpsimd", s_ve, s_ve.n)
        p.dma("gpsimd", d_hT.ap()[128 * tc:128 * tc + 128, :], hstage[:, :],
              s_go)
        p.wait("vector", s_go, s_go.n)
        p.wait("tensor", s_ve, s_ve.n)

    if debug_stop == "hT":
        dump_seq512(d_hT)
        return nc, p

    def load_cube_plain(src_dram, width):
        src = src_dram.ap().rearrange("(w gg uu) c -> w gg uu c", gg=4, uu=32)
        p.wait("sync", s_go, s_go.n)
        nst = (32 * width) // 4096
        for g in range(4):
            for st in range(nst):
                us = (4096 // width) * st
                p.dma("sync",
                      cube[32 * g:32 * g + 32, 4096 * st:4096 * st + 4096],
                      src[:, g, us:us + 4096 // width, :], s_ld)

    def f1_stage(d_out, roff, ioff):
        sent = {}
        for up in range(32):
            p.wait("tensor", s_ve, s_ve.n)
            for g in range(4):
                for pl, off in ((0, roff), (1, ioff)):
                    p.op("tensor", lambda e, up=up, g=g, pl=pl, off=off:
                         e.matmul(
                             PS[0:33, 1024 * g + 512 * pl:
                                1024 * g + 512 * pl + 512],
                             f1m_t[32 * g:32 * g + 32, off:off + 33],
                             cube[32 * g:32 * g + 32, 512 * up:512 * up + 512],
                             start=True, stop=True,
                             tile_position=(32 * g, 0)),
                         inc=(s_mm if (g == 3 and pl == 1) else None))
            p.wait("vector", s_mm, s_mm.n)
            for g in range(4):
                u_abs = 32 * g + up
                slot = g % 2
                if slot in sent:
                    p.wait("vector", s_go, s_go.n)
                p.op("vector", lambda e, slot=slot, g=g: e.tensor_copy(
                    out=spec[0:33, 1024 * slot:1024 * slot + 1024],
                    in_=PS[0:33, 1024 * g:1024 * g + 1024]),
                    inc=s_ve)
                p.wait("gpsimd", s_ve, s_ve.n)
                for pl in range(2):
                    p.dma("gpsimd", d_out.ap()[pl, :, u_abs, :],
                          spec[0:33, 1024 * slot + 512 * pl:
                               1024 * slot + 512 * pl + 512], s_go)
                sent[slot] = s_go.n
        p.wait("sync", s_go, s_go.n)

    f1m_t = sb("f1m", [128, 132], BF16)
    p.dma("sync", f1m_t[:, 0:33], ic["F1R"].ap()[:, :], s_ld)
    p.dma("sync", f1m_t[:, 33:66], ic["F1I"].ap()[:, :], s_ld)
    p.dma("sync", f1m_t[:, 66:99], ic["F1Rh"].ap()[:, :], s_ld)
    p.dma("sync", f1m_t[:, 99:132], ic["F1Ih"].ap()[:, :], s_ld)
    cwm_t = sb("cwm", [33, 64], BF16)
    p.dma("sync", cwm_t[:, 0:32], ic["CwR"].ap()[:, :], s_ld)
    p.dma("sync", cwm_t[:, 32:64], ic["CwI"].ap()[:, :], s_ld)

    load_cube_plain(d_hT, 512)
    p.wait("tensor", s_ld, s_ld.n)
    if debug_stop in ("hcube", "hcube2", "hcube3"):
        if debug_stop == "hcube3":
            p.wait("sync", s_cc, s_cc.n)
        p.wait("vector", s_ld, s_ld.n)
        for c in range(NC):
            p.op("vector", lambda e, c=c: e.tensor_copy(
                out=xf[c][:, :], in_=xh[c][:, :]), inc=None)
        p.op("vector", lambda e: e.tensor_copy(
            out=xf[0][:, 0:132], in_=f1m_t[:, :]), inc=s_ve)
        for rr in range(4):
            p.dma("sync", spec[:, 512 * rr:512 * rr + 512],
                  d_hT.ap()[128 * rr:128 * rr + 128, :], s_ld)
        p.wait("vector", s_ld, s_ld.n)
        p.op("vector", lambda e: e.tensor_copy(
            out=xf[1][:, :], in_=spec[:, :]), inc=s_ve)
        finish_output_from_xf()
        return nc, p
    f1_stage(d_Yh, 66, 99)
    bar()
    if debug_stop == "dYh":
        dump_kcube(d_Yh)
        return nc, p

    def f2_matmuls(gm, dstR, dstI, yR, yI):
        p.op("tensor", lambda e: e.matmul(
            dstR, gm[:, 0:128], yR, start=True, stop=False), inc=None)
        p.op("tensor", lambda e: e.matmul(
            dstR, gm[:, 128:256], yI, start=False, stop=True), inc=None)
        p.op("tensor", lambda e: e.matmul(
            dstI, gm[:, 256:384], yR, start=True, stop=False), inc=None)
        p.op("tensor", lambda e: e.matmul(
            dstI, gm[:, 0:128], yI, start=False, stop=True), inc=s_mm)

    gld = []
    for k in range(KH):
        gm = gmat[k % 2]
        if k >= 2:
            p.wait("sync", s_mm, gld[k - 2])
        p.dma("sync", gm[:, 0:128], ic["GRh"].ap()[k], s_ld)
        p.dma("sync", gm[:, 128:256], ic["GInegh"].ap()[k], s_ld)
        p.dma("sync", gm[:, 256:384], ic["GIh"].ap()[k], s_ld)
        yR = spec[:, 0:512] if k % 2 == 0 else spec[:, 1024:1536]
        yI = spec[:, 512:1024] if k % 2 == 0 else spec[:, 1536:2048]
        p.dma("sync", yR, d_Yh.ap()[0, k], s_ld)
        p.dma("sync", yI, d_Yh.ap()[1, k], s_ld)
        p.wait("tensor", s_ld, s_ld.n)
        if k >= 1:
            p.wait("tensor", s_ve, s_ve.n)
        f2_matmuls(gm, PS[:, 0:512], PS[:, 512:1024], yR, yI)
        gld.append(s_mm.n)
        p.wait("vector", s_mm, s_mm.n)
        p.op("vector", lambda e: e.tensor_copy(
            out=cw1[:, :], in_=PS[:, 0:512]), inc=None)
        p.op("vector", lambda e: e.tensor_copy(
            out=cw2[:, :], in_=PS[:, 512:1024]), inc=s_ve)
        p.wait("gpsimd", s_ve, s_ve.n)
        p.dma("gpsimd", d_H.ap()[0, k], cw1[:, :], s_go)
        p.dma("gpsimd", d_H.ap()[1, k], cw2[:, :], s_go)
        p.wait("vector", s_go, s_go.n)
    bar()
    if debug_stop == "dH":
        dump_kcube(d_H)
        return nc, p


    layernorm_to(xh)
    bar()
    if debug_stop == "ln1":
        for c in range(NC):
            p.op("vector", lambda e, c=c: e.tensor_copy(
                out=xf[c][:, :], in_=xh[c][:, :]), inc=s_ve)
        finish_output_from_xf()
        return nc, p

    load_weight(Wq)
    bar(("sync", "vector"), ("tensor",))
    if debug_stop == "qproj":
        def qraw_out(oc):
            p.wait("vector", s_mm, s_mm.n)
            p.op("vector", lambda e, oc=oc: e.tensor_scalar(
                out=qh[oc][:, :], in0=PS[:, 0:2048],
                scalar1=cols[:, oc:oc + 1], scalar2=None, op0=ALU.add),
                inc=s_ve)
        proj_chunks(xh, qraw_out)
        bar()
        for c in range(NC):
            p.op("vector", lambda e, c=c: e.tensor_copy(
                out=xf[c][:, :], in_=qh[c][:, :]), inc=s_ve)
        finish_output_from_xf()
        return nc, p
    proj_chunks(xh, lambda oc: head_ln_out(oc, 0, 0, 1, qh[oc]))

    load_weight(Wk)
    bar(("sync", "vector"), ("tensor",))
    kv_sent = []

    def k_out(oc):
        dst = kvst[oc % 2]
        if oc >= 2:
            p.wait("vector", s_go, s_go.n)
        head_ln_out(oc, 8, 2, 3, dst)
        p.wait("gpsimd", s_ve, s_ve.n)
        p.dma("gpsimd", d_K_in.ap()[128 * oc:128 * oc + 128, :], dst, s_go)
        kv_sent.append(s_go.n)
    proj_chunks(xh, k_out)

    load_weight(Wv)
    bar(("sync", "vector"), ("tensor",))

    def v_out(oc):
        dst = kvst[oc % 2]
        if len(kv_sent) >= 2:
            p.wait("vector", s_go, s_go.n)
        head_ln_out(oc, 16, 4, 5, dst)
        p.wait("gpsimd", s_ve, s_ve.n)
        p.dma("gpsimd", d_V_in.ap()[128 * oc:128 * oc + 128, :], dst, s_go)
        kv_sent.append(s_go.n)
    proj_chunks(xh, v_out)

    p.wait("gpsimd", s_go, s_go.n)
    if debug_stop != "hcube2":
        p.op("gpsimd", lambda e: e.collective_compute(
            "AllGather", ALU.bypass, replica_groups=PAIRS,
            ins=[d_K_in.ap().opt()], outs=[d_K_out.ap().opt()]), inc=s_cc)
        p.op("gpsimd", lambda e: e.collective_compute(
            "AllGather", ALU.bypass, replica_groups=PAIRS,
            ins=[d_V_in.ap().opt()], outs=[d_V_out.ap().opt()]), inc=s_cc)

    load_weight(Wo)
    bar()
    if debug_stop == "qkv":
        for c in range(NC):
            p.op("vector", lambda e, c=c: e.tensor_copy(
                out=xf[c][:, :], in_=qh[c][:, :]), inc=s_ve)
        finish_output_from_xf()
        return nc, p

    ones_vts = vts.rearrange("q (k o) -> q k o", o=65)[:, :, 64:65]
    p.op("gpsimd", lambda e: e.memset(ones_vts, 1.0), inc=sems["gpsimd"])
    for _eng in ("tensor", "vector", "sync"):
        p.wait(_eng, sems["gpsimd"], sems["gpsimd"].n)
    p.wait("sync", s_cc, s_cc.n)
    kq = d_K_out.ap().rearrange("(h p) t -> h p t", p=128)
    vq = d_V_out.ap().rearrange("(h p) t -> h p t", p=128)

    qk_group = []
    for hp in range(NH // 2):
        if hp > 0:
            p.wait("sync", s_vt, hp)
        p.dma("sync", vpair[:, 0:2048], vq[hp], s_ld)
        p.dma("sync", vpair[:, 2048:4096], vq[8 + hp], s_ld)
        for half in range(2):
            if hp > 0:
                p.wait("sync", s_qk, qk_group[-1])
            p.dma("sync", kbufH[half], kq[8 * half + hp], s_ld)
        vld = s_ld.n
        p.wait("tensor", s_ld, vld)
        if hp > 0:
            p.wait("tensor", s_exp, s_exp.n)
            p.wait("tensor", s_ve, s_ve.n)
        for hh in range(2):
            for kc in range(32):
                p.op("tensor", lambda e, hh=hh, kc=kc: e.transpose(
                    PSb[:, 128 * kc:128 * kc + 64],
                    vpair[64 * hh:64 * hh + 64, 128 * kc:128 * kc + 128],
                    id64x2[64 * hh:64 * hh + 64, :]),
                    inc=(s_mm if kc == 31 else None))
            p.wait("vector", s_mm, s_mm.n)
            p.op("vector", lambda e, hh=hh: e.tensor_copy(
                out=vts[:, 2080 * hh:2080 * hh + 2080].rearrange(
                    "q (k o) -> q k o", o=65)[:, :, 0:64],
                in_=PSb[:, 0:4096].rearrange("q (k o) -> q k o", o=128)[
                    :, :, 0:64]),
                inc=s_ve)
            p.wait("tensor", s_ve, s_ve.n)
        s_vt.n += 1
        p.ops["tensor"].append(lambda e, h=s_vt.h: e.sem_inc(h, 1))
        for qhf in range(2):
            qsl = slice(1024 * qhf, 1024 * qhf + 1024)
            p.wait("tensor", s_ve, s_ve.n)
            p.wait("tensor", s_go, s_go.n)
            for kc in range(33):
                if kc < 32:
                    half, kcl = kc // 16, kc % 16
                    ks = kbufH[half]
                    p.wait("tensor", s_exp, s_exp.n)
                    for hh in range(2):
                        for nb in range(2):
                            p.op("tensor", lambda e, hh=hh, nb=nb, qsl=qsl,
                                 ks=ks, kcl=kcl, hp=hp: e.matmul(
                                     PS[:, 1024 * hh + 512 * nb:
                                        1024 * hh + 512 * nb + 512],
                                     ks[64 * hh:64 * hh + 64,
                                        128 * kcl:128 * kcl + 128],
                                     qh[hp][64 * hh:64 * hh + 64, qsl][
                                         :, 512 * nb:512 * nb + 512],
                                     start=True, stop=True),
                                 inc=(s_qk if (hh == 1 and nb == 1) else None))
                    qk_group.append(s_qk.n)
                if kc > 0:
                    kp = kc - 1
                    for hh in range(2):
                        for nb in range(2):
                            p.op("tensor", lambda e, hh=hh, nb=nb, kp=kp:
                                 e.matmul(
                                     PS[0:65, 2048 + 1024 * hh:
                                        2048 + 1024 * hh + 1024][
                                         :, 512 * nb:512 * nb + 512],
                                     vts[:, 2080 * hh + 65 * kp:
                                         2080 * hh + 65 * kp + 65],
                                     pbuf[kp % 2][:, 1024 * hh + 512 * nb:
                                                  1024 * hh + 512 * nb + 512],
                                     start=(kp == 0), stop=(kp == 31)),
                                 inc=(s_pv if (hh == 1 and nb == 1) else None))
                if kc < 32:
                    p.wait("scalar", s_qk, s_qk.n)
                    p.wait("scalar", s_pv, s_pv.n - 1)
                    p.op("scalar", lambda e, kc=kc: e.activation(
                        out=pbuf[kc % 2], in_=PS[:, 0:2048], func=AF.Exp,
                        scale=0.125), inc=s_exp)
            p.wait("vector", s_pv, s_pv.n)
            p.op("vector", lambda e, qsl=qsl, hp=hp: e.tensor_copy(
                out=xh[hp][0:64, qsl], in_=PS[0:64, 2048:3072]), inc=None)
            p.op("vector", lambda e: e.tensor_copy(
                out=statb[0:64, 2048:3072], in_=PS[0:64, 3072:4096]),
                inc=None)
            for hh in range(2):
                p.op("vector", lambda e, hh=hh: e.tensor_copy(
                    out=stat[64:65, 1024 * hh:1024 * hh + 1024],
                    in_=PS[64:65, 2048 + 1024 * hh:2048 + 1024 * hh + 1024]),
                    inc=(s_ve if hh == 1 else None))
            p.wait("gpsimd", s_ve, s_ve.n)
            p.dma("gpsimd", xh[hp][64:128, qsl], statb[0:64, 2048:3072], s_go)
            p.dma("gpsimd", wstage[0:2, 0:1024], stat[64:65, 0:2048],
                  s_go)
            p.wait("vector", s_go, s_go.n)
            p.op("vector", lambda e, qhf=qhf: e.reciprocal(
                out=stat[0:2, 1024 * qhf:1024 * qhf + 1024],
                in_=wstage[0:2, 0:1024]), inc=s_ve)
        p.wait("tensor", s_ve, s_ve.n)
        p.wait("tensor", s_se, s_se.n)
        p.wait("tensor", s_exp, s_exp.n)
        for qhf in range(2):
            for nb in range(2):
                sl = slice(1024 * qhf + 512 * nb, 1024 * qhf + 512 * nb + 512)
                p.op("tensor", lambda e, sl=sl, qhf=qhf, nb=nb: e.matmul(
                    PS[:, 0:2048][:, sl], sel128_s[:, :],
                    stat[0:2, 1024 * qhf + 512 * nb:
                         1024 * qhf + 512 * nb + 512],
                    start=True, stop=True),
                    inc=(s_mm if (qhf == 1 and nb == 1) else None))
        p.wait("vector", s_mm, s_mm.n)
        p.op("vector", lambda e, hp=hp: e.tensor_tensor(
            out=PS[:, 2048:4096], in0=xh[hp][:, :], in1=PS[:, 0:2048],
            op=ALU.mult), inc=None)
        p.op("vector", lambda e, hp=hp: e.tensor_copy(
            out=xh[hp][:, :], in_=PS[:, 2048:4096]), inc=s_ve)
    bar()
    if debug_stop == "attn":
        for c in range(NC):
            p.op("vector", lambda e, c=c: e.tensor_copy(
                out=xf[c][:, :], in_=xh[c][:, :]), inc=s_ve)
        finish_output_from_xf()
        return nc, p

    proj_chunks(xh, lambda oc: residual_add(oc, cols[:, 24 + oc:25 + oc]))
    bar()
    if debug_stop == "attno":
        finish_output_from_xf()
        return nc, p
    if debug_stop == "ffnd":
        for c in range(NC):
            p.op("vector", lambda e, c=c: e.memset(xf[c][:, :], 0.0), inc=None)
        p.op("vector", lambda e: e.tensor_copy(
            out=xf[0][:, 0:32], in_=small[:, 0:32]), inc=s_ve)
        finish_output_from_xf()
        return nc, p
    if debug_stop == "ffnw":
        for c in range(NC):
            p.op("vector", lambda e, c=c: e.memset(xf[c][:, :], 0.0), inc=None)
            p.op("vector", lambda e, c=c: e.tensor_copy(
                out=xf[c][:, 0:1024], in_=wTc[c][:, :]), inc=s_ve)
        finish_output_from_xf()
        return nc, p

    layernorm_to(xh)
    bar()
    if debug_stop == "ln2":
        for c in range(NC):
            p.op("vector", lambda e, c=c: e.tensor_copy(
                out=xf[c][:, :], in_=xh[c][:, :]), inc=s_ve)
        finish_output_from_xf()
        return nc, p
    proj_chunks(xh, lambda oc: residual_add(
        oc, cols[:, 56 + oc:57 + oc], alpha_col=small[:, 6:7]), wbuf=wTc)
    bar()
    if debug_stop == "ffn1":
        finish_output_from_xf()
        return nc, p

    p.dma("gpsimd", bvb_row[:, :],
          bvb.ap().unsqueeze(0).broadcast_to([128, D]), s_go)
    p.wait("vector", s_go, s_go.n)
    load_weight(Wvb)
    layernorm_to(xh)
    bar()
    if debug_stop == "ln3":
        for c in range(NC):
            p.op("vector", lambda e, c=c: e.tensor_copy(
                out=xf[c][:, :], in_=xh[c][:, :]), inc=s_ve)
        finish_output_from_xf()
        return nc, p
    for tc in range(16):
        p.wait("tensor", s_ve, s_ve.n)
        for nb in range(2):
            for icc in range(NC):
                p.op("tensor", lambda e, tc=tc, nb=nb, icc=icc: e.matmul(
                    PS[:, 0:1024][:, 512 * nb:512 * nb + 512],
                    xh[icc][:, 128 * tc:128 * tc + 128],
                    wAc[icc][:, 512 * nb:512 * nb + 512],
                    start=(icc == 0), stop=(icc == NC - 1)),
                    inc=(s_mm if (icc == NC - 1 and nb == 1) else None))
        p.wait("vector", s_mm, s_mm.n)
        p.op("vector", lambda e: e.tensor_copy(
            out=vstage[:, :], in_=PS[:, 0:1024]), inc=None)
        p.op("vector", lambda e: e.tensor_tensor(
            out=ATT[:, 2048:3072], in0=vstage[:, :], in1=bvb_row[:, :],
            op=ALU.add), inc=s_ve)
        p.wait("gpsimd", s_ve, s_ve.n)
        p.dma("gpsimd", d_vhin.ap()[128 * tc:128 * tc + 128, :],
              ATT[:, 2048:3072], s_go)
        p.wait("vector", s_go, s_go.n)
    p.wait("gpsimd", s_go, s_go.n)
    if debug_stop == "vdram":
        p.wait("sync", s_go, s_go.n)
        for tt in range(16):
            p.dma("sync", spec[:, 0:1024],
                  d_vhin.ap()[128 * tt:128 * tt + 128, :], s_ld)
            p.wait("vector", s_ld, s_ld.n)
            p.op("vector", lambda e: e.tensor_copy(
                out=stg[:, 0:1024], in_=spec[:, 0:1024]), inc=s_ve)
            p.wait("gpsimd", s_ve, s_ve.n)
            p.dma("gpsimd", out_t.ap()[128 * tt:128 * tt + 128, :],
                  stg[:, 0:1024], s_go)
            p.wait("sync", s_go, s_go.n)
        p.wait("gpsimd", s_go, s_go.n)
        return nc, p
    p.op("gpsimd", lambda e: e.collective_compute(
        "AllGather", ALU.bypass, replica_groups=PAIRS,
        ins=[d_vhin.ap().opt()], outs=[d_vhout.ap().opt()]), inc=s_cc)

    load_weight(Wu)
    bar(("sync", "vector"), ("tensor",))

    def u_out(oc):
        p.wait("vector", s_mm, s_mm.n)
        p.op("vector", lambda e, oc=oc: e.tensor_scalar(
            out=qh[oc][:, :], in0=PS[:, 0:2048],
            scalar1=cols[:, 32 + oc:33 + oc], scalar2=None, op0=ALU.add),
            inc=s_ve)
    proj_chunks(xh, u_out)
    bar()
    if debug_stop == "vproj":
        finish_output_from_xf()
        return nc, p

    p.wait("sync", s_cc, s_cc.n)
    vsrc = d_vhout.ap().rearrange("(w gg uu) c -> w gg uu c", gg=4, uu=32)
    cmb_done = []
    for g in range(4):
        rows = slice(32 * g, 32 * g + 32)
        for st4 in range(4):
            sl = slice(4096 * st4, 4096 * st4 + 4096)
            j = 4 * g + st4
            if j >= 1:
                p.wait("sync", s_ve, cmb_done[j - 1])
            p.dma("sync", cstA[rows, :], vsrc[:, g, 8 * st4:8 * st4 + 8,
                                              0:512], s_ld)
            p.dma("sync", cstB[rows, :], vsrc[:, g, 8 * st4:8 * st4 + 8,
                                              512:1024], s_ld)
            p.wait("vector", s_ld, s_ld.n)
            p.op("vector", lambda e, rows=rows, g=g: e.tensor_scalar(
                out=PS[32 * g:32 * g + 32, 0:4096], in0=cstA[rows, :],
                scalar1=small[rows, 7:8], scalar2=None, op0=ALU.mult),
                inc=None)
            p.op("vector", lambda e, rows=rows, sl=sl, g=g:
                 e.scalar_tensor_tensor(
                     out=cube[rows, :][:, sl], in0=cstB[rows, :],
                     scalar=small[rows, 8:9],
                     in1=PS[32 * g:32 * g + 32, 0:4096],
                     op0=ALU.mult, op1=ALU.add), inc=s_ve)
            cmb_done.append(s_ve.n)
    if debug_stop == "cube":
        bar()
        for c in range(NC):
            p.op("vector", lambda e, c=c: e.tensor_copy(
                out=xf[c][:, :], in_=xh[c][:, :]), inc=s_ve)
        finish_output_from_xf()
        return nc, p
    p.wait("tensor", s_ve, s_ve.n)
    f1_stage(d_Yv, 0, 33)
    bar()
    if debug_stop == "dYv":
        dump_kcube(d_Yv)
        return nc, p

    gld2 = []
    for k in range(KH):
        gm = gmat[k % 2]
        if k >= 2:
            p.wait("sync", s_mm, gld2[k - 2])
        p.dma("sync", gm[:, 0:128], ic["GR"].ap()[k], s_ld)
        p.dma("sync", gm[:, 128:256], ic["GIneg"].ap()[k], s_ld)
        p.dma("sync", gm[:, 256:384], ic["GI"].ap()[k], s_ld)
        p.dma("sync", gm[:, 384:512], ic["G2R"].ap()[k], s_ld)
        p.dma("sync", gm[:, 512:640], ic["G2Ineg"].ap()[k], s_ld)
        p.dma("sync", gm[:, 640:768], ic["G2I"].ap()[k], s_ld)
        yR = spec[:, 0:512] if k % 2 == 0 else spec[:, 1024:1536]
        yI = spec[:, 512:1024] if k % 2 == 0 else spec[:, 1536:2048]
        p.dma("sync", yR, d_Yv.ap()[0, k], s_ld)
        p.dma("sync", yI, d_Yv.ap()[1, k], s_ld)
        hR = cstA[:, 512 * (k % 2):512 * (k % 2) + 512]
        hI = cstA[:, 1024 + 512 * (k % 2):1024 + 512 * (k % 2) + 512]
        p.dma("sync", hR, d_H.ap()[0, k], s_ld)
        p.dma("sync", hI, d_H.ap()[1, k], s_ld)
        p.wait("tensor", s_ld, s_ld.n)
        if k >= 1:
            p.wait("tensor", s_ve, s_ve.n)
        p.op("tensor", lambda e, gm=gm, yR=yR, yI=yI: e.matmul(
            PS[:, 0:512], gm[:, 0:128], yR, start=True, stop=False), inc=None)
        p.op("tensor", lambda e, gm=gm, yR=yR, yI=yI: e.matmul(
            PS[:, 0:512], gm[:, 128:256], yI, start=False, stop=True),
            inc=None)
        p.op("tensor", lambda e, gm=gm, yR=yR, yI=yI: e.matmul(
            PS[:, 512:1024], gm[:, 256:384], yR, start=True, stop=False),
            inc=None)
        p.op("tensor", lambda e, gm=gm, yR=yR, yI=yI: e.matmul(
            PS[:, 512:1024], gm[:, 0:128], yI, start=False, stop=True),
            inc=s_mm)
        p.wait("vector", s_mm, s_mm.n)
        p.op("vector", lambda e, hR=hR: e.scalar_tensor_tensor(
            out=cw3[:, :], in0=PS[:, 0:512], scalar=1.0, in1=hR,
            op0=ALU.mult, op1=ALU.mult), inc=None)
        p.op("vector", lambda e, hI=hI: e.scalar_tensor_tensor(
            out=cw4[:, :], in0=PS[:, 512:1024], scalar=1.0, in1=hI,
            op0=ALU.mult, op1=ALU.mult), inc=None)
        p.op("vector", lambda e: e.tensor_tensor(
            out=cw1[:, :], in0=cw3[:, :], in1=cw4[:, :], op=ALU.subtract),
            inc=None)
        p.op("vector", lambda e, hI=hI: e.scalar_tensor_tensor(
            out=cw3[:, :], in0=PS[:, 0:512], scalar=1.0, in1=hI,
            op0=ALU.mult, op1=ALU.mult), inc=None)
        p.op("vector", lambda e, hR=hR: e.scalar_tensor_tensor(
            out=cw4[:, :], in0=PS[:, 512:1024], scalar=1.0, in1=hR,
            op0=ALU.mult, op1=ALU.mult), inc=None)
        p.op("vector", lambda e: e.tensor_tensor(
            out=cw2[:, :], in0=cw3[:, :], in1=cw4[:, :], op=ALU.add),
            inc=s_ve)
        p.wait("tensor", s_ve, s_ve.n)
        p.op("tensor", lambda e, gm=gm: e.matmul(
            PS[:, 1024:1536], gm[:, 384:512], cw1[:, :], start=True,
            stop=False), inc=None)
        p.op("tensor", lambda e, gm=gm: e.matmul(
            PS[:, 1024:1536], gm[:, 512:640], cw2[:, :], start=False,
            stop=True), inc=None)
        p.op("tensor", lambda e, gm=gm: e.matmul(
            PS[:, 1536:2048], gm[:, 640:768], cw1[:, :], start=True,
            stop=False), inc=None)
        p.op("tensor", lambda e, gm=gm: e.matmul(
            PS[:, 1536:2048], gm[:, 384:512], cw2[:, :], start=False,
            stop=True), inc=s_mm)
        gld2.append(s_mm.n)
        p.wait("vector", s_mm, s_mm.n)
        p.op("vector", lambda e: e.tensor_copy(
            out=cw3[:, :], in_=PS[:, 1024:1536]), inc=None)
        p.op("vector", lambda e: e.tensor_copy(
            out=cw4[:, :], in_=PS[:, 1536:2048]), inc=s_ve)
        p.wait("gpsimd", s_ve, s_ve.n)
        p.dma("gpsimd", d_B.ap()[0, k], cw3[:, :], s_go)
        p.dma("gpsimd", d_B.ap()[1, k], cw4[:, :], s_go)
        p.wait("vector", s_go, s_go.n)
    bar()
    if debug_stop == "dB":
        dump_kcube(d_B)
        return nc, p

    bload = []
    ysent = []
    evdone = []
    for u in range(128):
        bt = bsp[u % 2]
        if u >= 2:
            p.wait("sync", s_mm, bload[u - 2])
        p.dma("sync", bt[0:33, 0:512], d_B.ap()[0, :, u, :], s_ld)
        p.dma("sync", bt[0:33, 512:1024], d_B.ap()[1, :, u, :], s_ld)
        p.wait("tensor", s_ld, s_ld.n)
        if u >= 2:
            p.wait("tensor", s_ve, evdone[u - 2])
        psl = PS[:, 2048 + 1024 * (u % 2):2048 + 1024 * (u % 2) + 512]
        p.op("tensor", lambda e, bt=bt, psl=psl: e.matmul(
            psl[0:32, :], cwm_t[:, 0:32], bt[0:33, 0:512],
            start=True, stop=False), inc=None)
        p.op("tensor", lambda e, bt=bt, psl=psl: e.matmul(
            psl[0:32, :], cwm_t[:, 32:64], bt[0:33, 512:1024],
            start=False, stop=True), inc=s_mm)
        bload.append(s_mm.n)
        p.wait("vector", s_mm, s_mm.n)
        cwd = cw1 if u % 2 == 0 else cw2
        if u >= 2:
            p.wait("vector", s_go, s_go.n)
        p.op("vector", lambda e, psl=psl, cwd=cwd: e.tensor_copy(
            out=cwd[0:32, :], in_=psl[0:32, :]), inc=s_ve)
        evdone.append(s_ve.n)
        p.wait("gpsimd", s_ve, s_ve.n)
        p.dma("gpsimd",
              d_y_in.ap().rearrange("(w q) c -> q w c", q=128)[u],
              cwd[0:32, :], s_go)
        ysent.append(s_go.n)
    p.wait("gpsimd", s_go, s_go.n)
    if debug_stop == "yin":
        dump_seq512(d_y_in)
        return nc, p
    p.op("gpsimd", lambda e: e.collective_compute(
        "AllGather", ALU.bypass, replica_groups=PAIRS,
        ins=[d_y_in.ap().opt()], outs=[d_y_out.ap().opt()]), inc=s_cc)

    load_weight(Wob)
    bar()

    p.wait("sync", s_cc, s_cc.n)
    for cc in range(NC):
        blk, ccol = cc // 4, 128 * (cc % 4)
        ysl = d_y_out.ap().rearrange("(b w q) c -> b w q c", b=2, q=128)
        p.wait("sync", s_ve, s_ve.n)
        p.dma("sync", cstA[:, 0:2048].rearrange("q (w c) -> q w c", c=128),
              ysl[blk, 0:16, :, ccol:ccol + 128].transpose([1, 0, 2]), s_ld)
        p.dma("sync", cstB[:, 0:2048].rearrange("q (w c) -> q w c", c=128),
              ysl[blk, 16:32, :, ccol:ccol + 128].transpose([1, 0, 2]), s_ld)
        p.wait("vector", s_ld, s_ld.n)
        p.wait("vector", s_mm, s_mm.n)
        p.op("vector", lambda e: e.tensor_scalar(
            out=PS[:, 0:2048], in0=cstA[:, 0:2048], scalar1=small[:, 7:8],
            scalar2=None, op0=ALU.mult), inc=None)
        p.op("vector", lambda e: e.scalar_tensor_tensor(
            out=spec[:, :], in0=cstB[:, 0:2048], scalar=small[:, 8:9],
            in1=PS[:, 0:2048], op0=ALU.mult, op1=ALU.add), inc=s_ve)
        p.wait("tensor", s_ve, s_ve.n)
        for tt in range(16):
            p.op("tensor", lambda e, tt=tt: e.transpose(
                PSb[:, 4096 + 256 * (tt % 8):4096 + 256 * (tt % 8) + 128],
                spec[:, 128 * tt:128 * tt + 128], ident_b[:, :]),
                inc=(s_mm if tt % 8 == 7 else None))
            if tt % 8 == 7:
                p.wait("vector", s_mm, s_mm.n)
                for t2 in range(tt - 7, tt + 1):
                    p.op("vector", lambda e, cc=cc, t2=t2: e.tensor_copy(
                        out=xh[cc][:, 128 * t2:128 * t2 + 128],
                        in_=PSb[:, 4096 + 256 * (t2 % 8):
                                4096 + 256 * (t2 % 8) + 128]),
                        inc=(s_ve if t2 == tt else None))
                p.wait("tensor", s_ve, s_ve.n)
    bar()
    if debug_stop == "ycomb":
        for c in range(NC):
            p.op("vector", lambda e, c=c: e.tensor_copy(
                out=xf[c][:, :], in_=xh[c][:, :]), inc=s_ve)
        finish_output_from_xf()
        return nc, p
    for c in range(NC):
        p.op("vector", lambda e, c=c: e.tensor_tensor(
            out=PS[:, 0:2048], in0=qh[c][:, :], in1=xh[c][:, :],
            op=ALU.mult), inc=None)
        p.op("vector", lambda e, c=c: e.tensor_copy(
            out=xh[c][:, :], in_=PS[:, 0:2048]), inc=s_ve)
    bar()
    if debug_stop == "conv":
        for c in range(NC):
            p.op("vector", lambda e, c=c: e.tensor_copy(
                out=xf[c][:, :], in_=xh[c][:, :]), inc=s_ve)
        finish_output_from_xf()
        return nc, p

    proj_chunks(xh, lambda oc: residual_add(oc, cols[:, 48 + oc:49 + oc]))
    bar()

    layernorm_to(xh)
    bar()
    proj_chunks(xh, lambda oc: residual_add(
        oc, cols[:, 56 + oc:57 + oc], alpha_col=small[:, 6:7]), wbuf=wTc)

    finish_output_from_xf()
    return nc, p


_CACHE = {}


def _get_program():
    key = (_DEBUG_STOP, _HL_TAP, _DUMP_OFF)
    if key not in _CACHE:
        nc, p = build_program(debug_stop=_DEBUG_STOP)
        with nc.Block() as block:
            for eng_name in ENGINES:
                opslist = p.ops[eng_name]
                getattr(block, eng_name)(
                    lambda e, ops=opslist: [f(e) for f in ops])
        _CACHE[key] = nc
    return _CACHE[key]


def kernel(**inputs):
    nc = _get_program()
    f32 = np.float32

    def arr(name):
        return np.ascontiguousarray(np.asarray(inputs[name], dtype=f32))

    x = arr("L0_attn_a_x")
    base = {
        "Wq": arr("L0_attn_a_Wq"), "Wk": arr("L0_attn_a_Wk"),
        "Wv": arr("L0_attn_a_Wv"), "Wo": arr("L0_attn_a_Wo"),
        "bq": arr("L0_attn_a_bq"), "bk": arr("L0_attn_a_bk"),
        "bv": arr("L0_attn_a_bv"), "bo": arr("L0_attn_a_bo"),
        "q_scale": arr("L0_attn_a_q_scale"), "q_bias": arr("L0_attn_a_q_bias"),
        "k_scale": arr("L0_attn_a_k_scale"), "k_bias": arr("L0_attn_a_k_bias"),
        "v_scale": arr("L0_attn_a_v_scale"), "v_bias": arr("L0_attn_a_v_bias"),
        "Wu": arr("L0_attn_b_Wu"), "Wvb": arr("L0_attn_b_Wv"),
        "Wob": arr("L0_attn_b_Wo"),
        "bu": arr("L0_attn_b_bu"), "bvb": arr("L0_attn_b_bv"),
        "bob": arr("L0_attn_b_bo"),
        "W1": arr("L0_attn_b_W1"), "W2": arr("L0_attn_b_W2"),
        "b1": arr("L0_attn_b_b1"), "b2": arr("L0_attn_b_b2"),
        "Wf": arr("L0_ffn_weight"), "f_alpha": arr("L0_ffn_alpha"),
        "f_bias": arr("L0_ffn_bias"),
    }
    W3 = arr("L0_attn_b_W3")
    b3 = arr("L0_attn_b_b3")
    in_maps = []
    for c in range(8):
        b, h = c // 2, c % 2
        m = dict(base)
        m["x_slab"] = np.ascontiguousarray(x[b, T * h:T * h + T, :])
        m["W3my"] = np.ascontiguousarray(W3[:, 512 * h:512 * h + 512])
        m["b3my"] = np.ascontiguousarray(b3[512 * h:512 * h + 512])
        m["sel"] = np.array([1.0 - h, float(h)], dtype=f32)
        in_maps.append(m)
    res = run_bass_kernel_spmd(nc, in_maps, core_ids=list(range(8)))
    out = np.empty((B, S, D), dtype=f32)
    for c in range(8):
        b, h = c // 2, c % 2
        out[b, T * h:T * h + T, :] = res.results[c]["out"]
    return out
